# version 02 (final)
# baseline (speedup 1.0000x reference)
import sys
sys.path.insert(0, "/opt/trn_rl_repo")
from contextlib import ExitStack

import numpy as np
import ml_dtypes

import concourse.bass as bass
import concourse.mybir as mybir
from concourse.bass_utils import run_bass_kernel_spmd

F32 = mybir.dt.float32
BF16 = mybir.dt.bfloat16
AF = mybir.ActivationFunctionType
ALU = mybir.AluOpType

D = 2048
NT = 16
H = 8192
NST = 16
RDT = 64
F = 512
LN_EPS = 1e-5
RMS_EPS = float(np.finfo(np.float32).eps)
RECIP_TH = 1024.0 / 513.0
CLIP_LO = float(ml_dtypes.bfloat16(1e-8))

ENGINES = ("sync", "tensor", "vector", "scalar", "gpsimd")


class Emitter:

    def __init__(self, nc, check=False):
        self.nc = nc
        self.check = check
        self.audit = []
        self.opseq = 0
        self.sems = {}
        self.counts = {}
        self.ready = {}
        self.snap = None
        self.guar_by_engine = {e: {} for e in ENGINES}
        self.cur = None
        self.eng = None
        self.guar = None

    def add_sem(self, es, name):
        self.sems[name] = es.enter_context(self.nc.semaphore(name))
        self.counts[name] = 0

    LOAD_SEMS = ["L%d" % i for i in range(16)] + ["LDC"]
    STORE_SEMS = ["S0", "S1", "SS"]

    def snapshot(self):
        self.snap = (dict(self.counts), dict(self.ready), self.opseq)

    def begin_pass(self, engine, eng_obj):
        self.counts, self.ready = dict(self.snap[0]), dict(self.snap[1])
        self.opseq = self.snap[2]
        self.cur = engine
        self.eng = eng_obj
        self.guar = self.guar_by_engine[engine]

    def op(self, engine, fn, deps=(), sem=None, inc=0, produces=(),
           is_dma=False):
        mine = engine == self.cur
        if mine:
            for k in deps:
                s, v = self.ready[k]
                if self.guar.get(s, 0) < v:
                    self.eng.wait_ge(self.sems[s], v)
                    self.guar[s] = v
        inst = fn(self.eng) if mine else None
        if self.check and mine:
            self.audit.append((self.opseq, engine, inst,
                               [self.ready[k] for k in deps],
                               sem if inc else None, inc, is_dma))
        self.opseq += 1
        if inc:
            self.counts[sem] += inc
            val = self.counts[sem]
            if mine:
                inst.then_inc(self.sems[sem], inc)
                if not is_dma:
                    self.guar[sem] = val
            for k in produces:
                self.ready[k] = (sem, val)
        else:
            assert not produces

    def alias(self, newkey, oldkey):
        self.ready[newkey] = self.ready[oldkey]

    def barrier(self):
        vals = {}
        for e in ENGINES:
            s = "bar_" + e
            self.counts[s] += 1
            vals[e] = self.counts[s]
        if self.cur == "sync":
            for s in self.LOAD_SEMS:
                if self.guar.get(s, 0) < self.counts[s]:
                    self.eng.wait_ge(self.sems[s], self.counts[s])
                    self.guar[s] = self.counts[s]
        if self.cur == "scalar":
            for s in self.STORE_SEMS:
                if self.guar.get(s, 0) < self.counts[s]:
                    self.eng.wait_ge(self.sems[s], self.counts[s])
                    self.guar[s] = self.counts[s]
        me = "bar_" + self.cur
        self.eng.sem_inc(self.sems[me], 1)
        self.guar[me] = vals[self.cur]
        for e in ENGINES:
            if e != self.cur:
                self.eng.wait_ge(self.sems["bar_" + e], vals[e])
                self.guar["bar_" + e] = vals[e]
        if self.check:
            drains = []
            if self.cur == "sync":
                drains = [(s, self.counts[s]) for s in self.LOAD_SEMS]
            if self.cur == "scalar":
                drains = [(s, self.counts[s]) for s in self.STORE_SEMS]
            self.audit.append((self.opseq, self.cur, "BARRIER",
                               [("bar_" + e, vals[e]) for e in ENGINES]
                               + drains,
                               "bar_" + self.cur, 1, False))
        self.opseq += 1


def build(dbg=False, check=False):
    nc = bass.Bass("TRN2", target_bir_lowering=False, debug=False,
                   detect_race_conditions=False)

    xT_pre = nc.dram_tensor("xT_pre", [D, 2048], F32, kind="ExternalInput").ap()
    xT_own = nc.dram_tensor("xT_own", [D, 2048], F32, kind="ExternalInput").ap()
    wguT = nc.dram_tensor("wguT", [D, 2 * H], F32, kind="ExternalInput").ap()
    wpT = nc.dram_tensor("wpT", [H, D], F32, kind="ExternalInput").ap()
    wxT3 = nc.dram_tensor("wxT3", [128, NT, 96], F32, kind="ExternalInput").ap()
    wdtT = nc.dram_tensor("wdtT", [RDT, D], F32, kind="ExternalInput").ap()
    alog3 = nc.dram_tensor("alog3", [128, 256], F32, kind="ExternalInput").ap()
    d3 = nc.dram_tensor("d3", [128, NT], F32, kind="ExternalInput").ap()
    bdt3 = nc.dram_tensor("bdt3", [128, NT], F32, kind="ExternalInput").ap()
    flagbc = nc.dram_tensor("flagbc", [128, 256], F32, kind="ExternalInput").ap()
    selsum = nc.dram_tensor("selsum", [128, 16, 128], BF16,
                            kind="ExternalInput").ap()
    selbc = nc.dram_tensor("selbc", [32, 16, 128], BF16,
                           kind="ExternalInput").ap()
    outT = nc.dram_tensor("outT", [D, 2048], F32, kind="ExternalOutput").ap()

    skind = "ExternalOutput" if dbg else "Internal"
    wqg_t = nc.dram_tensor("wqg_t", [64, 2, 16, 128, 128], BF16, kind=skind).ap()
    wpp_t = nc.dram_tensor("wpp_t", [16, 64, 128, 128], BF16, kind=skind).ap()
    s_t = nc.dram_tensor("s_t", [NT, 4, 128, F], F32, kind=skind).ap()

    with ExitStack() as es:
        em = Emitter(nc, check=check)
        sems = (Emitter.LOAD_SEMS + Emitter.STORE_SEMS +
                ["CT", "CV", "CS", "CP"] +
                ["bar_" + e for e in ENGINES])
        for s in sems:
            em.add_sem(es, s)

        wxt_sb = es.enter_context(nc.sbuf_tensor("wxt_sb", [128, NT, 96], F32))
        a_sb = es.enter_context(nc.sbuf_tensor("a_sb", [128, 256], F32))
        d_sb = es.enter_context(nc.sbuf_tensor("d_sb", [128, NT], F32))
        bdt_sb = es.enter_context(nc.sbuf_tensor("bdt_sb", [128, NT], F32))
        flag_sb = es.enter_context(nc.sbuf_tensor("flag_sb", [128, 256], F32))
        carry = es.enter_context(nc.sbuf_tensor("carry", [128, 256], F32))
        ones_col = es.enter_context(nc.sbuf_tensor("ones_col", [128, 1], F32))
        ones_row = es.enter_context(nc.sbuf_tensor("ones_row", [1, 128], F32))
        ones_r32 = es.enter_context(nc.sbuf_tensor("ones_r32", [33, 128], F32))
        ones_cbf = es.enter_context(nc.sbuf_tensor("ones_cbf", [128, 1], BF16))
        rows_a = es.enter_context(nc.sbuf_tensor("rows_a", [33, 4096], F32))
        stgb = es.enter_context(nc.sbuf_tensor("stgb", [1, F], F32))
        stgc = es.enter_context(nc.sbuf_tensor("stgc", [1, F], F32))
        mb_sb = es.enter_context(nc.sbuf_tensor("mb_sb", [128, F], F32))
        epsln_c = es.enter_context(nc.sbuf_tensor("epsln_c", [1, 1], F32))
        epsrm_c = es.enter_context(nc.sbuf_tensor("epsrm_c", [1, 1], F32))
        rb_sb = es.enter_context(nc.sbuf_tensor("rb_sb", [128, F], F32))
        pb = [es.enter_context(nc.psum_tensor(f"pb{i}", [128, F], F32))
              for i in range(8)]

        def alloc_p0(pes):
            return None

        def phase_p0(em, bufs):
            loads = [(wxt_sb, wxT3), (a_sb, alog3),
                     (d_sb, d3), (bdt_sb, bdt3), (flag_sb, flagbc)]
            nld = len(loads)
            for i, (dst, src) in enumerate(loads):
                em.op("sync", lambda e, d=dst, s=src:
                      e.dma_start(out=d[:], in_=s),
                      sem="LDC", inc=16,
                      produces=([("p0done",)] if i == nld - 1 else []),
                      is_dma=True)
            em.op("vector", lambda e: e.memset(ones_col[:], 1.0),
                  sem="CV", inc=1, produces=[("ones",)])
            em.op("vector", lambda e: e.memset(ones_row[:], 1.0),
                  sem="CV", inc=1, produces=[("onesr",)])
            em.op("vector", lambda e: e.memset(ones_r32[32:33, :], 1.0),
                  sem="CV", inc=1, produces=[("onesr32",)])
            em.op("vector", lambda e: e.memset(ones_cbf[:], 1.0),
                  sem="CV", inc=1, produces=[("onesb",)])
            em.op("vector", lambda e: e.memset(carry[:], 0.0),
                  sem="CV", inc=1, produces=[("carry0",)])
            em.op("vector", lambda e: e.memset(epsln_c[:], LN_EPS),
                  sem="CV", inc=1, produces=[("epsln",)])
            em.op("vector", lambda e: e.memset(epsrm_c[:], RMS_EPS),
                  sem="CV", inc=1, produces=[("epsrm",)])
            em.op("scalar", lambda e: e.activation(out=a_sb[:], in_=a_sb[:],
                                                   func=AF.Exp),
                  deps=[("p0done",)], sem="CS", inc=1, produces=[("aexp",)])
            em.op("vector", lambda e: e.tensor_scalar_mul(
                out=a_sb[:], in0=a_sb[:], scalar1=-1.0),
                deps=[("aexp",)], sem="CV", inc=1, produces=[("aneg",)])

        def alloc_p1(pes):
            return {
                "xab": [pes.enter_context(
                    nc.sbuf_tensor(f"p1x{i}", [128, F], F32)) for i in range(2)],
                "sqb": [pes.enter_context(
                    nc.sbuf_tensor(f"p1s{i}", [128, F], F32)) for i in range(2)],
                "rows_b": pes.enter_context(
                    nc.sbuf_tensor("rows_b", [33, 4096], F32)),
                "rowtmp": pes.enter_context(
                    nc.sbuf_tensor("rowtmp", [1, 4096], F32)),
                "rowtmp2": pes.enter_context(
                    nc.sbuf_tensor("rowtmp2", [1, 4096], F32)),
            }

        def phase_p1(em, bufs):
            xab, sqb = bufs["xab"], bufs["sqb"]
            rows_b, rowtmp = bufs["rows_b"], bufs["rowtmp"]
            rowtmp2 = bufs["rowtmp2"]
            for seg in range(2):
                src = xT_pre if seg == 0 else xT_own
                for tblk in range(4):
                    for tau in range(NT):
                        i = (seg * 4 + tblk) * NT + tau
                        j = i % 2
                        em.op("sync",
                              lambda e, s=src, t=tau, tb=tblk, j=j:
                              e.dma_start(out=xab[j][:],
                                          in_=s[t * 128:(t + 1) * 128,
                                               tb * F:(tb + 1) * F]),
                              deps=([("p1mm", i - 2)] if i >= 2 else []),
                              sem="L%d" % (i % 2), inc=16,
                              produces=[("p1x", i)], is_dma=True)
                        em.op("scalar",
                              lambda e, j=j: e.activation(
                                  out=sqb[j][:], in_=xab[j][:], func=AF.Square),
                              deps=[("p1x", i)] +
                              ([("p1mm", i - 2)] if i >= 2 else []),
                              sem="CS", inc=1, produces=[("p1sq", i)])
                        bankdeps = [("p1ev", seg * 4 + tblk - 1)] \
                            if (tau == 0 and seg * 4 + tblk >= 1) else []
                        em.op("tensor",
                              lambda e, j=j, t=tau: e.matmul(
                                  out=pb[0][0:1, :], lhsT=ones_col[:],
                                  rhs=xab[j][:], start=(t == 0), stop=False,
                                  skip_group_check=True),
                              deps=[("p1x", i), ("ones",)] + bankdeps)
                        em.op("tensor",
                              lambda e, j=j, t=tau: e.matmul(
                                  out=pb[7][0:1, :], lhsT=ones_col[:],
                                  rhs=sqb[j][:], start=(t == 0),
                                  stop=(t == NT - 1), skip_group_check=True),
                              deps=[("p1sq", i)], sem="CT", inc=1,
                              produces=[("p1mm", i)])
                    g = seg * 2048 + tblk * F
                    em.op("scalar",
                          lambda e, gg=g: e.copy(out=rows_b[0:1, gg:gg + F],
                                                 in_=pb[0][0:1, :]),
                          deps=[("p1mm", (seg * 4 + tblk) * NT + NT - 1)],
                          sem="CS", inc=0)
                    em.op("scalar",
                          lambda e, gg=g: e.copy(out=rows_b[32:33, gg:gg + F],
                                                 in_=pb[7][0:1, :]),
                          sem="CS", inc=1, produces=[("p1ev", seg * 4 + tblk)])
            em.op("scalar", lambda e: e.activation(
                out=rows_a[0:1, :], in_=rows_b[0:1, :], func=AF.Copy,
                scale=1.0 / 2048.0),
                deps=[("p1ev", 7)], sem="CS", inc=1, produces=[("meanr",)])
            em.op("vector", lambda e: e.tensor_tensor(
                out=rowtmp[:], in0=rows_a[0:1, :], in1=rows_a[0:1, :],
                op=ALU.mult),
                deps=[("meanr",)], sem="CV", inc=1, produces=[("m2",)])
            em.op("scalar", lambda e: e.copy(out=rowtmp2[:],
                                             in_=rows_b[32:33, :]),
                deps=[("p1ev", 7)], sem="CS", inc=1, produces=[("ssq0",)])
            em.op("vector", lambda e: e.scalar_tensor_tensor(
                out=rowtmp[:], in0=rowtmp2[:], scalar=1.0 / 2048.0,
                in1=rowtmp[:], op0=ALU.mult, op1=ALU.subtract),
                deps=[("m2",), ("ssq0",)], sem="CV", inc=1,
                produces=[("var",)])
            em.op("scalar", lambda e: e.activation(
                out=rowtmp[:], in_=rowtmp[:], func=AF.Ln, bias=epsln_c[:]),
                deps=[("var",), ("epsln",)], sem="CS", inc=1,
                produces=[("lnv",)])
            em.op("scalar", lambda e: e.activation(
                out=rows_a[32:33, :], in_=rowtmp[:], func=AF.Exp, scale=-0.5),
                deps=[("lnv",)], sem="CS", inc=1, produces=[("rstdr",)])

        def alloc_p2(pes):
            return {
                "ss": pes.enter_context(
                    nc.sbuf_tensor("ss_sb", [128, 16, 128], BF16)),
                "sbm": pes.enter_context(
                    nc.sbuf_tensor("sb_sb", [32, 16, 128], BF16)),
                "wb": [pes.enter_context(
                    nc.sbuf_tensor(f"p2w{i}", [128, F], F32))
                    for i in range(16)],
                "ab": [pes.enter_context(
                    nc.sbuf_tensor(f"p2a{i}", [128, F], BF16))
                    for i in range(16)],
                "s32": pes.enter_context(nc.sbuf_tensor("p2s32", [32, F], BF16)),
                "mb": [pes.enter_context(
                    nc.sbuf_tensor(f"p2m{i}", [128, F], BF16))
                    for i in range(2)],
                "sg": [pes.enter_context(
                    nc.sbuf_tensor(f"p2g{i}", [128, F], BF16))
                    for i in range(2)],
                "tb": [pes.enter_context(
                    nc.sbuf_tensor(f"p2t{i}", [128, F], BF16))
                    for i in range(2)],
                "wq": [pes.enter_context(
                    nc.sbuf_tensor(f"p2q{i}", [128, F], BF16))
                    for i in range(2)],
            }

        def phase_p2(em, bufs):
            ss_sb, sb_sb, s32 = bufs["ss"], bufs["sbm"], bufs["s32"]
            wb, ab, mb, sgb, tbb, wqb = (bufs["wb"], bufs["ab"], bufs["mb"],
                                         bufs["sg"], bufs["tb"], bufs["wq"])
            em.op("sync", lambda e: e.dma_start(out=ss_sb[:], in_=selsum),
                  sem="LDC", inc=16, is_dma=True)
            em.op("sync", lambda e: e.dma_start(out=sb_sb[:], in_=selbc),
                  sem="LDC", inc=16, produces=[("seldone",)], is_dma=True)

            chunks = []
            for kt in range(16):
                for hc in range(32):
                    g, c4 = hc // 16, (hc % 16) * 4
                    chunks.append((
                        wguT[kt * 128:(kt + 1) * 128, hc * F:(hc + 1) * F],
                        wqg_t[c4:c4 + 4, g, kt, :, :].rearrange("c p h -> p c h")))
            for kc in range(64):
                for dc in range(4):
                    chunks.append((
                        wpT[kc * 128:(kc + 1) * 128, dc * F:(dc + 1) * F],
                        wpp_t[dc * 4:dc * 4 + 4, kc, :, :].rearrange(
                            "t p h -> p t h")))
            nbatch = len(chunks) // 16

            def gprev(B, j):
                return (B, j - 2) if j >= 2 else (B - 1, 14 + j)

            for B in range(nbatch):
                for j in range(16):
                    src, _ = chunks[B * 16 + j]
                    em.op("sync",
                          lambda e, s=src, j=j: e.dma_start(out=wb[j][:], in_=s),
                          deps=([("p2sgn", B - 1, j)] if B >= 1 else []),
                          sem="L%d" % j, inc=16, produces=[("p2w", B, j)],
                          is_dma=True)
                    em.op("scalar",
                          lambda e, j=j: e.activation(out=ab[j][:],
                                                      in_=wb[j][:], func=AF.Abs),
                          deps=[("p2w", B, j)] +
                          ([("p2m", B - 1, j)] if B >= 1 else []),
                          sem="CS", inc=1, produces=[("p2a", B, j)])
                    deps = [("p2a", B, j), ("seldone",)]
                    if j == 0 and B >= 1:
                        deps.append(("p2s32", B - 1))
                    em.op("tensor",
                          lambda e, j=j: e.matmul(
                              out=pb[0][:], lhsT=ss_sb[:, j, :], rhs=ab[j][:],
                              start=(j == 0), stop=(j == 15),
                              skip_group_check=True),
                          deps=deps, sem="CT", inc=(1 if j == 15 else 0),
                          produces=([("p2ssum", B)] if j == 15 else []))
                deps = [("p2ssum", B)]
                if B >= 1:
                    deps.append(("p2bc", B - 1, 15))
                em.op("scalar",
                      lambda e: e.activation(out=s32[:], in_=pb[0][0:32, :],
                                             func=AF.Copy, scale=1.0 / 64.0),
                      deps=deps, sem="CS", inc=1, produces=[("p2s32a", B)])
                em.op("vector",
                      lambda e: e.tensor_scalar_max(out=s32[:], in0=s32[:],
                                                    scalar1=CLIP_LO),
                      deps=[("p2s32a", B)], sem="CV", inc=1,
                      produces=[("p2s32", B)])
                for j in range(16):
                    _, dst = chunks[B * 16 + j]
                    bank = pb[1 + (j % 2)]
                    first = (B == 0 and j < 2)
                    pj = gprev(B, j)
                    deps = [("p2s32", B), ("seldone",)]
                    if not first:
                        deps.append(("p2wq",) + pj)
                    em.op("tensor",
                          lambda e, j=j, bank=bank: e.matmul(
                              out=bank[:], lhsT=sb_sb[:, j, :], rhs=s32[:],
                              start=True, stop=True, skip_group_check=True),
                          deps=deps, sem="CT", inc=1, produces=[("p2bc", B, j)])
                    deps = [("p2bc", B, j)]
                    if not first:
                        deps.append(("p2tt",) + pj)
                    em.op("vector",
                          lambda e, j=j, bank=bank: e.scalar_tensor_tensor(
                              out=mb[j % 2][:], in0=ab[j][:], scalar=RECIP_TH,
                              in1=bank[:], op0=ALU.mult, op1=ALU.is_gt),
                          deps=deps, sem="CV", inc=1, produces=[("p2m", B, j)])
                    deps = [("p2w", B, j)]
                    if not first:
                        deps.append(("p2tt",) + pj)
                    em.op("scalar",
                          lambda e, j=j: e.sign(out=sgb[j % 2][:],
                                                in_=wb[j][:]),
                          deps=deps, sem="CS", inc=1, produces=[("p2sgn", B, j)])
                    deps = [("p2m", B, j), ("p2sgn", B, j)]
                    if not first:
                        deps.append(("p2wq",) + pj)
                    em.op("gpsimd",
                          lambda e, j=j: e.tensor_tensor(
                              out=tbb[j % 2][:], in0=mb[j % 2][:],
                              in1=sgb[j % 2][:], op=ALU.mult),
                          deps=deps, sem="CP", inc=1, produces=[("p2tt", B, j)])
                    deps = [("p2tt", B, j)]
                    if not first:
                        deps.append(("p2st",) + pj)
                    em.op("vector",
                          lambda e, j=j, bank=bank: e.tensor_tensor(
                              out=wqb[j % 2][:], in0=tbb[j % 2][:], in1=bank[:],
                              op=ALU.mult),
                          deps=deps, sem="CV", inc=1, produces=[("p2wq", B, j)])
                    em.op("scalar",
                          lambda e, j=j, d=dst: e.dma_start(
                              out=d, in_=wqb[j % 2][:].rearrange(
                                  "p (c h) -> p c h", h=128)),
                          deps=[("p2wq", B, j)], sem="S%d" % (j % 2), inc=16,
                          produces=[("p2st", B, j)], is_dma=True)

        def alloc_p3(pes):
            return {
                "xabA": [pes.enter_context(
                    nc.sbuf_tensor(f"p3xA{i}", [128, F], F32)) for i in range(2)],
                "xnbA": [pes.enter_context(
                    nc.sbuf_tensor(f"p3nA{i}", [128, F], F32)) for i in range(2)],
                "delta": pes.enter_context(
                    nc.sbuf_tensor("p3d", [128, NT, F], F32)),
                "du": pes.enter_context(nc.sbuf_tensor("p3u", [128, NT, F], F32)),
                "hbuf": pes.enter_context(
                    nc.sbuf_tensor("p3h", [128, NT, F], F32)),
                "yacc": pes.enter_context(
                    nc.sbuf_tensor("p3y", [128, NT, F], F32)),
                "xp": pes.enter_context(nc.sbuf_tensor("p3xp", [64, F], F32)),
                "wdtb": [pes.enter_context(
                    nc.sbuf_tensor(f"p3wdt{i}", [RDT, 128], F32))
                    for i in range(2)],
                "xpbc": pes.enter_context(
                    nc.sbuf_tensor("p3xpbc", [32, F], F32)),
                "bbc": [pes.enter_context(
                    nc.sbuf_tensor(f"p3b{i}", [128, F], F32)) for i in range(2)],
                "cbc": [pes.enter_context(
                    nc.sbuf_tensor(f"p3c{i}", [128, F], F32)) for i in range(2)],
                "sc1": [pes.enter_context(
                    nc.sbuf_tensor(f"p3s1{i}", [128, F], F32)) for i in range(2)],
                "sc2": [pes.enter_context(
                    nc.sbuf_tensor(f"p3s2{i}", [128, F], F32)) for i in range(2)],
                "ab2": [pes.enter_context(
                    nc.sbuf_tensor(f"p3ab{i}", [128, F], F32)) for i in range(2)],
                "bx2": [pes.enter_context(
                    nc.sbuf_tensor(f"p3bx{i}", [128, F], F32)) for i in range(2)],
                "ym2": [pes.enter_context(
                    nc.sbuf_tensor(f"p3ym{i}", [128, F], F32)) for i in range(2)],
            }

        def phase_p3(em, bufs):
            xabA, xnbA = bufs["xabA"], bufs["xnbA"]
            xabB, xnbB = xabA, xnbA
            delta, du, hbuf, yacc = (bufs["delta"], bufs["du"], bufs["hbuf"],
                                     bufs["yacc"])
            xp_sb, xp_bc = bufs["xp"], bufs["xpbc"]
            wdtb = bufs["wdtb"]
            bbc, cbc = bufs["bbc"], bufs["cbc"]
            sc1, sc2, ab2, bx2, ym2 = (bufs["sc1"], bufs["sc2"], bufs["ab2"],
                                       bufs["bx2"], bufs["ym2"])
            for tc in range(8):
                src = xT_pre if tc < 4 else xT_own
                tcol = (tc % 4) * F
                g = tc * F
                with_y = tc >= 4
                if tc == 4:
                    em.op("vector",
                          lambda e: e.tensor_tensor(
                              out=carry[:], in0=carry[:], in1=flag_sb[:],
                              op=ALU.mult),
                          deps=[("p0done",)], sem="CV", inc=1,
                          produces=[("flagged",)])
                em.op("tensor",
                      lambda e, gg=g: e.matmul(
                          out=pb[5][:], lhsT=ones_row[:],
                          rhs=rows_a[0:1, gg:gg + F], start=True, stop=True,
                          skip_group_check=True),
                      deps=[("meanr",), ("onesr",)] +
                      ([("mbev", tc - 1)] if tc else []),
                      sem="CT", inc=1, produces=[("mbps", tc)])
                em.op("tensor",
                      lambda e, gg=g: e.matmul(
                          out=pb[6][:], lhsT=ones_r32[32:33, :],
                          rhs=rows_a[32:33, gg:gg + F], start=True, stop=True,
                          skip_group_check=True),
                      deps=[("rstdr",), ("onesr32",)] +
                      ([("rbev", tc - 1)] if tc else []),
                      sem="CT", inc=1, produces=[("rbps", tc)])
                deps = [("mbps", tc)]
                if tc:
                    deps.append(("mbrb_free", tc - 1))
                em.op("scalar", lambda e: e.copy(out=mb_sb[:], in_=pb[5][:]),
                      deps=deps, sem="CS", inc=1,
                      produces=[("mbev", tc), ("mb", tc)])
                deps = [("rbps", tc)]
                if tc:
                    deps.append(("mbrb_free", tc - 1))
                em.op("scalar", lambda e: e.copy(out=rb_sb[:], in_=pb[6][:]),
                      deps=deps, sem="CS", inc=1,
                      produces=[("rbev", tc), ("rb", tc)])

                for tau in range(NT):
                    j = tau % 2
                    deps = [("p3xsA", tc, tau - 2)] if tau >= 2 else (
                        [("p3xsB", tc - 1, 14 + tau)] if tc else [])
                    em.op("sync",
                          lambda e, s=src, t=tau, co=tcol, j=j:
                          e.dma_start(out=xabA[j][:],
                                      in_=s[t * 128:(t + 1) * 128, co:co + F]),
                          deps=deps, sem="L%d" % j, inc=16,
                          produces=[("p3xA", tc, tau)], is_dma=True)
                    deps = [("p3xA", tc, tau), ("mb", tc)]
                    if tau >= 2:
                        deps.append(("p3xpmm", tc, tau - 2))
                    elif tc:
                        deps.append(("p3xpmm", tc - 1, 14 + tau))
                    em.op("vector",
                          lambda e, j=j: e.tensor_tensor(
                              out=xnbA[j][:], in0=xabA[j][:], in1=mb_sb[:],
                              op=ALU.subtract),
                          deps=deps, sem="CV", inc=1,
                          produces=[("p3xsA", tc, tau)])
                    em.op("vector",
                          lambda e, j=j: e.tensor_tensor(
                              out=xnbA[j][:], in0=xnbA[j][:], in1=rb_sb[:],
                              op=ALU.mult),
                          deps=[("rb", tc)], sem="CV", inc=1,
                          produces=[("p3xnA", tc, tau)])
                    if with_y:
                        deps = [("p3xnA", tc, tau)]
                        if tc > 4:
                            deps.append(("s_st", tc - 1))
                        em.op("vector",
                              lambda e, t=tau, j=j: e.tensor_scalar_mul(
                                  out=yacc[:, t, :], in0=xnbA[j][:],
                                  scalar1=d_sb[:, t:t + 1]),
                              deps=deps, sem="CV", inc=1,
                              produces=[("yini", tc, tau)])
                    deps = [("p3xnA", tc, tau)]
                    if tau == 0 and tc:
                        deps.append(("xpev", tc - 1))
                    em.op("tensor",
                          lambda e, t=tau, j=j: e.matmul(
                              out=pb[0][0:96, :], lhsT=wxt_sb[:, t, :],
                              rhs=xnbA[j][:], start=(t == 0), stop=(t == NT - 1),
                              skip_group_check=True),
                          deps=deps, sem="CT", inc=1,
                          produces=[("p3xpmm", tc, tau)] +
                          ([("xpps", tc)] if tau == NT - 1 else []))
                deps = [("xpps", tc)]
                if tc:
                    deps.append(("cps", tc - 1, 15))
                em.op("scalar", lambda e: e.copy(out=xp_sb[:],
                                                 in_=pb[0][0:64, :]),
                      deps=deps, sem="CS", inc=0)
                em.op("scalar", lambda e: e.copy(out=xp_bc[:],
                                                 in_=pb[0][64:96, :]),
                      sem="CS", inc=1, produces=[("xpev", tc), ("xp", tc)])

                for tau in range(NT):
                    j = tau % 2
                    bank = pb[1 + j]
                    deps = [("dps", tc, tau - 2)] if tau >= 2 else (
                        [("dps", tc - 1, 14 + tau)] if tc else [])
                    em.op("sync",
                          lambda e, t=tau, j=j: e.dma_start(
                              out=wdtb[j][:],
                              in_=wdtT[:, t * 128:(t + 1) * 128]),
                          deps=deps, sem="L%d" % (2 + j), inc=16,
                          produces=[("wdt", tc, tau)], is_dma=True)
                    deps = [("xp", tc), ("wdt", tc, tau)]
                    if tau >= 2:
                        deps.append(("dps_free", tc, tau - 2))
                    elif tc:
                        deps.append(("dps_free", tc - 1, 14 + tau))
                    em.op("tensor",
                          lambda e, t=tau, j=j, bank=bank: e.matmul(
                              out=bank[:], lhsT=wdtb[j][:],
                              rhs=xp_sb[:], start=True, stop=True,
                              skip_group_check=True),
                          deps=deps, sem="CT", inc=1, produces=[("dps", tc, tau)])
                    deps = [("dps", tc, tau)]
                    if tau >= 2:
                        deps.append(("spadd", tc, tau - 2))
                    elif tc:
                        deps.append(("spadd", tc - 1, 14 + tau))
                    em.op("scalar",
                          lambda e, t=tau, j=j, bank=bank: e.activation(
                              out=sc1[j][:], in_=bank[:], func=AF.Abs,
                              bias=bdt_sb[:, t:t + 1]),
                          deps=deps, sem="CS", inc=1)
                    em.op("scalar",
                          lambda e, j=j: e.activation(
                              out=sc1[j][:], in_=sc1[j][:], func=AF.Exp,
                              scale=-1.0),
                          sem="CS", inc=0)
                    em.op("scalar",
                          lambda e, j=j: e.activation(
                              out=sc1[j][:], in_=sc1[j][:], func=AF.Ln,
                              bias=1.0),
                          sem="CS", inc=0)
                    em.op("scalar",
                          lambda e, t=tau, j=j, bank=bank: e.activation(
                              out=sc2[j][:], in_=bank[:], func=AF.Relu,
                              bias=bdt_sb[:, t:t + 1]),
                          sem="CS", inc=1,
                          produces=[("sp4", tc, tau), ("dps_free", tc, tau)])
                    deps = [("sp4", tc, tau)]
                    if tc:
                        deps.append(("abar_l", tc - 1, tau))
                    em.op("vector",
                          lambda e, t=tau, j=j: e.tensor_tensor(
                              out=delta[:, t, :], in0=sc1[j][:], in1=sc2[j][:],
                              op=ALU.add),
                          deps=deps, sem="CV", inc=1,
                          produces=[("delta", tc, tau), ("spadd", tc, tau)])
                    deps = [("p3xsB", tc, tau - 2)] if tau >= 2 else                         [("p3xsA", tc, 14 + tau)]
                    em.op("sync",
                          lambda e, s=src, t=tau, co=tcol, j=j:
                          e.dma_start(out=xabB[j][:],
                                      in_=s[t * 128:(t + 1) * 128, co:co + F]),
                          deps=deps, sem="L%d" % j, inc=16,
                          produces=[("p3xB", tc, tau)], is_dma=True)
                    deps = [("p3xB", tc, tau), ("mb", tc)]
                    if tau >= 2:
                        deps.append(("du", tc, tau - 2))
                    else:
                        deps.append(("p3xpmm", tc, 14 + tau))
                        if with_y:
                            deps.append(("yini", tc, 14 + tau))
                    em.op("gpsimd",
                          lambda e, j=j: e.tensor_tensor(
                              out=xnbB[j][:], in0=xabB[j][:], in1=mb_sb[:],
                              op=ALU.subtract),
                          deps=deps, sem="CP", inc=1,
                          produces=[("p3xsB", tc, tau)])
                    em.op("vector",
                          lambda e, j=j: e.tensor_tensor(
                              out=xnbB[j][:], in0=xnbB[j][:], in1=rb_sb[:],
                              op=ALU.mult),
                          deps=[("p3xsB", tc, tau), ("rb", tc)],
                          sem="CV", inc=1, produces=[("p3xnB", tc, tau)])
                    deps = [("p3xnB", tc, tau), ("delta", tc, tau)]
                    if tc:
                        deps.append(("bx_l", tc - 1, tau))
                    em.op("vector",
                          lambda e, t=tau, j=j: e.tensor_tensor(
                              out=du[:, t, :], in0=delta[:, t, :],
                              in1=xnbB[j][:], op=ALU.mult),
                          deps=deps, sem="CV", inc=1,
                          produces=[("du", tc, tau)])
                em.alias(("mbrb_free", tc), ("du", tc, NT - 1))

                for n in range(NST):
                    n2 = n % 2
                    deps = [("xp", tc)]
                    if n >= 1:
                        deps.append(("bps", tc, n - 1))
                    elif tc:
                        deps.append(("bps", tc - 1, NST - 1))
                    em.op("sync",
                          lambda e, nn=n: e.dma_start(
                              out=stgb[:], in_=xp_bc[nn:nn + 1, :]),
                          deps=deps, sem="L4", inc=16,
                          produces=[("bstg", tc, n)], is_dma=True)
                    deps = [("xp", tc)]
                    if n >= 1:
                        deps.append(("cps", tc, n - 1))
                    elif tc:
                        deps.append(("cps", tc - 1, NST - 1))
                    em.op("sync",
                          lambda e, nn=n: e.dma_start(
                              out=stgc[:], in_=xp_bc[NST + nn:NST + nn + 1, :]),
                          deps=deps, sem="L5", inc=16,
                          produces=[("cstg", tc, n)], is_dma=True)
                    deps = [("bstg", tc, n), ("onesr",)]
                    if n >= 1:
                        deps.append(("bbcev", tc, n - 1))
                    elif tc:
                        deps.append(("bbcev", tc - 1, NST - 1))
                    em.op("tensor",
                          lambda e: e.matmul(
                              out=pb[3][:], lhsT=ones_row[:],
                              rhs=stgb[:], start=True, stop=True,
                              skip_group_check=True),
                          deps=deps, sem="CT", inc=1, produces=[("bps", tc, n)])
                    deps = [("cstg", tc, n)]
                    if n >= 1:
                        deps.append(("cbcev", tc, n - 1))
                    elif tc:
                        deps.append(("cbcev", tc - 1, NST - 1))
                    em.op("tensor",
                          lambda e: e.matmul(
                              out=pb[4][:], lhsT=ones_row[:],
                              rhs=stgc[:], start=True, stop=True,
                              skip_group_check=True),
                          deps=deps, sem="CT", inc=1, produces=[("cps", tc, n)])
                    deps = [("bps", tc, n)]
                    if n >= 2:
                        deps.append(("bxn_l", tc, n - 2))
                    elif tc:
                        deps.append(("bxn_l", tc - 1, 14 + n))
                    em.op("scalar",
                          lambda e, j=n2: e.copy(out=bbc[j][:], in_=pb[3][:]),
                          deps=deps, sem="CS", inc=1,
                          produces=[("bbc", tc, n), ("bbcev", tc, n)])
                    deps = [("cps", tc, n)]
                    if with_y:
                        if n >= 2:
                            deps.append(("ymn_l", tc, n - 2))
                        elif tc > 4:
                            deps.append(("ymn_l", tc - 1, 14 + n))
                    em.op("scalar",
                          lambda e, j=n2: e.copy(out=cbc[j][:], in_=pb[4][:]),
                          deps=deps, sem="CS", inc=1,
                          produces=[("cbc", tc, n), ("cbcev", tc, n)])
                    for tau in range(NT):
                        idx = n * NT + tau
                        p2 = idx % 2
                        deps = [("delta", tc, tau), ("aneg",)]
                        if idx >= 2:
                            deps.append(("tts", tc, idx - 2))
                        elif tc:
                            deps.append(("tts", tc - 1, 254 + idx))
                        em.op("scalar",
                              lambda e, t=tau, nn=n, j=p2: e.activation(
                                  out=ab2[j][:], in_=delta[:, t, :],
                                  func=AF.Exp,
                                  scale=a_sb[:, t * 16 + nn:t * 16 + nn + 1]),
                              deps=deps, sem="CS", inc=1,
                              produces=[("abar", tc, n, tau)] +
                              ([("abar_l", tc, tau)] if n == NST - 1 else []))
                        deps = [("du", tc, tau), ("bbc", tc, n)]
                        if idx >= 2:
                            deps.append(("tts", tc, idx - 2))
                        elif tc:
                            deps.append(("tts", tc - 1, 254 + idx))
                        em.op("gpsimd",
                              lambda e, t=tau, j=p2, nn=n2: e.tensor_tensor(
                                  out=bx2[j][:], in0=du[:, t, :],
                                  in1=bbc[nn][:], op=ALU.mult),
                              deps=deps, sem="CP", inc=1,
                              produces=[("bx", tc, n, tau)] +
                              ([("bx_l", tc, tau)] if n == NST - 1 else []) +
                              ([("bxn_l", tc, n)] if tau == NT - 1 else []))
                        deps = [("abar", tc, n, tau), ("bx", tc, n, tau)]
                        if tc == 0 and idx == 0:
                            deps.append(("carry0",))
                        if tc == 4 and idx == 0:
                            deps.append(("flagged",))
                        if with_y and n >= 1:
                            deps.append(("ym", tc, n - 1, tau))
                        elif with_y and tc > 4 and n == 0:
                            deps.append(("ym", tc - 1, NST - 1, tau))
                        em.op("vector",
                              lambda e, t=tau, nn=n, j=p2: e.tensor_tensor_scan(
                                  out=hbuf[:, t, :], data0=ab2[j][:],
                                  data1=bx2[j][:],
                                  initial=carry[:, t * 16 + nn:t * 16 + nn + 1],
                                  op0=ALU.mult, op1=ALU.add),
                              deps=deps, sem="CV", inc=1,
                              produces=[("tts", tc, idx), ("h", tc, n, tau)])
                        if with_y:
                            deps = [("h", tc, n, tau), ("cbc", tc, n)]
                            if idx >= 2:
                                deps.append(("yac", tc, idx - 2))
                            elif tc > 4:
                                deps.append(("yac", tc - 1, 254 + idx))
                            em.op("gpsimd",
                                  lambda e, t=tau, j=p2, nn=n2: e.tensor_tensor(
                                      out=ym2[j][:], in0=hbuf[:, t, :],
                                      in1=cbc[nn][:], op=ALU.mult),
                                  deps=deps, sem="CP", inc=1,
                                  produces=[("ym", tc, n, tau)] +
                                  ([("ymn_l", tc, n)] if tau == NT - 1 else []))
                            deps = [("ym", tc, n, tau)]
                            if n == 0:
                                deps.append(("yini", tc, tau))
                            em.op("vector",
                                  lambda e, t=tau, j=p2: e.tensor_tensor(
                                      out=yacc[:, t, :], in0=yacc[:, t, :],
                                      in1=ym2[j][:], op=ALU.add),
                                  deps=deps, sem="CV", inc=1,
                                  produces=[("yac", tc, idx)] +
                                  ([("yacT", tc, tau)] if n == NST - 1 else []))
                    em.op("vector",
                          lambda e, nn=n: e.tensor_copy(
                              carry[:].rearrange("p (t n) -> p t n", n=16)
                              [:, :, nn:nn + 1],
                              hbuf[:, :, F - 1:F]),
                          sem="CV", inc=0)
                if with_y:
                    for tau in range(NT):
                        em.op("scalar",
                              lambda e, t=tau, tb=tc - 4: e.dma_start(
                                  out=s_t[t, tb, :, :], in_=yacc[:, t, :]),
                              deps=[("yacT", tc, tau)], sem="SS", inc=16,
                              produces=([("s_st", tc)] if tau == NT - 1
                                        else []), is_dma=True)

        def alloc_p4(pes):
            return {
                "xab": [pes.enter_context(
                    nc.sbuf_tensor(f"p4x{i}", [128, F], F32)) for i in range(2)],
                "xns": [pes.enter_context(
                    nc.sbuf_tensor(f"p4n{i}", [128, F], F32)) for i in range(2)],
                "xnhi": pes.enter_context(
                    nc.sbuf_tensor("p4hi", [128, NT, F], BF16)),
                "xnlo": pes.enter_context(
                    nc.sbuf_tensor("p4lo", [128, NT, F], BF16)),
                "wg": [pes.enter_context(
                    nc.sbuf_tensor(f"p4wg{i}", [128, 2, 16, 128], BF16))
                    for i in range(2)],
                "wp": [pes.enter_context(
                    nc.sbuf_tensor(f"p4wp{i}", [128, 16, 128], BF16))
                    for i in range(2)],
                "hhi": pes.enter_context(
                    nc.sbuf_tensor("p4shi", [128, 16, F], BF16)),
                "hlo": pes.enter_context(
                    nc.sbuf_tensor("p4slo", [128, 16, F], BF16)),
                "gacc": pes.enter_context(
                    nc.sbuf_tensor("p4g", [128, NT, F], F32)),
                "sil": [pes.enter_context(
                    nc.sbuf_tensor(f"p4si{i}", [128, F], F32)) for i in range(2)],
                "hfb": [pes.enter_context(
                    nc.sbuf_tensor(f"p4hf{i}", [128, F], F32)) for i in range(2)],
                "sqb": [pes.enter_context(
                    nc.sbuf_tensor(f"p4sq{i}", [128, F], BF16))
                    for i in range(2)],
                "rb4": pes.enter_context(nc.sbuf_tensor("p4rb", [128, F], F32)),
                "gsil": [pes.enter_context(
                    nc.sbuf_tensor(f"p4gs{i}", [128, F], F32)) for i in range(2)],
                "ssb": [pes.enter_context(
                    nc.sbuf_tensor(f"p4ss{i}", [128, F], F32)) for i in range(2)],
                "t1b": [pes.enter_context(
                    nc.sbuf_tensor(f"p4t1{i}", [128, F], F32)) for i in range(2)],
                "outb": [pes.enter_context(
                    nc.sbuf_tensor(f"p4o{i}", [128, F], F32)) for i in range(2)],
            }

        def phase_p4(em, bufs):
            xab, xns = bufs["xab"], bufs["xns"]
            xnhi, xnlo, wg, wp = (bufs["xnhi"], bufs["xnlo"], bufs["wg"],
                                  bufs["wp"])
            hsthi, hstlo, gacc = bufs["hhi"], bufs["hlo"], bufs["gacc"]
            silb, hfb, sqb, rb4 = (bufs["sil"], bufs["hfb"], bufs["sqb"],
                                   bufs["rb4"])
            gsil, ssb, t1b, outb = (bufs["gsil"], bufs["ssb"], bufs["t1b"],
                                    bufs["outb"])
            for tb in range(4):
                g = 2048 + tb * F
                em.op("tensor",
                      lambda e, gg=g: e.matmul(
                          out=pb[5][:], lhsT=ones_row[:],
                          rhs=rows_a[0:1, gg:gg + F], start=True, stop=True,
                          skip_group_check=True),
                      deps=[("meanr",)] +
                      ([("mbev4", tb - 1), ("sil", tb - 1, 63)] if tb else []),
                      sem="CT", inc=1, produces=[("mbps4", tb)])
                em.op("tensor",
                      lambda e, gg=g: e.matmul(
                          out=pb[6][:], lhsT=ones_r32[32:33, :],
                          rhs=rows_a[32:33, gg:gg + F], start=True, stop=True,
                          skip_group_check=True),
                      deps=[("rstdr",), ("onesr32",)] +
                      ([("rbev4", tb - 1), ("hmul", tb - 1, 63)] if tb else []),
                      sem="CT", inc=1, produces=[("rbps4", tb)])
                deps = [("mbps4", tb)]
                if tb:
                    deps.append(("xn4s", tb - 1, NT - 1))
                em.op("scalar", lambda e: e.copy(out=mb_sb[:], in_=pb[5][:]),
                      deps=deps, sem="CS", inc=1,
                      produces=[("mbev4", tb), ("mb4", tb)])
                deps = [("rbps4", tb)]
                if tb:
                    deps.append(("xn4", tb - 1, NT - 1))
                em.op("scalar", lambda e: e.copy(out=rb_sb[:], in_=pb[6][:]),
                      deps=deps, sem="CS", inc=1,
                      produces=[("rbev4", tb), ("rb4", tb)])
                for tau in range(NT):
                    j = tau % 2
                    deps = [("xn4s", tb, tau - 2)] if tau >= 2 else (
                        [("comb_x", tb - 1, 14 + tau)] if tb else [])
                    em.op("sync",
                          lambda e, t=tau, co=tb * F, j=j:
                          e.dma_start(out=xab[j][:],
                                      in_=xT_own[t * 128:(t + 1) * 128,
                                                 co:co + F]),
                          deps=deps, sem="L%d" % j, inc=16,
                          produces=[("x4", tb, tau)], is_dma=True)
                    deps = [("x4", tb, tau), ("mb4", tb)]
                    if tau >= 2:
                        deps.append(("xn4", tb, tau - 2))
                    em.op("vector",
                          lambda e, j=j: e.tensor_tensor(
                              out=xns[j][:], in0=xab[j][:], in1=mb_sb[:],
                              op=ALU.subtract),
                          deps=deps, sem="CV", inc=1,
                          produces=[("xn4s", tb, tau)])
                    em.op("vector",
                          lambda e, j=j: e.tensor_tensor(
                              out=xns[j][:], in0=xns[j][:], in1=rb_sb[:],
                              op=ALU.mult),
                          deps=[("rb4", tb)], sem="CV", inc=1)
                    deps = []
                    if tb:
                        deps.append(("mm1dn", tb - 1))
                    em.op("vector",
                          lambda e, t=tau, j=j: e.tensor_copy(
                              xnhi[:, t, :], xns[j][:]),
                          deps=deps, sem="CV", inc=0)
                    em.op("vector",
                          lambda e, t=tau, j=j: e.tensor_tensor(
                              out=xnlo[:, t, :], in0=xns[j][:],
                              in1=xnhi[:, t, :], op=ALU.subtract),
                          sem="CV", inc=1, produces=[("xn4", tb, tau)])

                for q in range(4):
                    for cc in range(16):
                        c = q * 16 + cc
                        j = c % 2
                        deps = [("wgfree", tb, c - 2)] if c >= 2 else (
                            [("wgfree", tb - 1, 62 + c)] if tb else [])
                        em.op("sync",
                              lambda e, c=c, j=j: e.dma_start(
                                  out=wg[j][:],
                                  in_=wqg_t[c, :, :, :, :].rearrange(
                                      "g k p h -> p g k h")),
                              deps=deps, sem="L%d" % (2 + j), inc=16,
                              produces=[("wg", tb, c)], is_dma=True)
                        deps = [("wg", tb, c), ("xn4", tb, NT - 1)]
                        if c >= 1:
                            deps.append(("sil", tb, c - 1))
                        elif tb:
                            deps.append(("sil", tb - 1, 63))
                        for k in range(NT):
                            em.op("tensor",
                                  lambda e, j=j, k=k: e.matmul(
                                      out=pb[0][:], lhsT=wg[j][:, 0, k, :],
                                      rhs=xnhi[:, k, :], start=(k == 0),
                                      stop=False, skip_group_check=True),
                                  deps=(deps if k == 0 else []))
                            em.op("tensor",
                                  lambda e, j=j, k=k: e.matmul(
                                      out=pb[0][:], lhsT=wg[j][:, 0, k, :],
                                      rhs=xnlo[:, k, :], start=False,
                                      stop=(k == NT - 1),
                                      skip_group_check=True),
                                  sem="CT", inc=(1 if k == NT - 1 else 0),
                                  produces=([("gps", tb, c)]
                                            if k == NT - 1 else []))
                        deps2 = []
                        if c >= 1:
                            deps2.append(("hmul", tb, c - 1))
                        elif tb:
                            deps2.append(("hmul", tb - 1, 63))
                        for k in range(NT):
                            em.op("tensor",
                                  lambda e, j=j, k=k: e.matmul(
                                      out=pb[1][:], lhsT=wg[j][:, 1, k, :],
                                      rhs=xnhi[:, k, :], start=(k == 0),
                                      stop=False, skip_group_check=True),
                                  deps=(deps2 if k == 0 else []))
                            em.op("tensor",
                                  lambda e, j=j, k=k: e.matmul(
                                      out=pb[1][:], lhsT=wg[j][:, 1, k, :],
                                      rhs=xnlo[:, k, :], start=False,
                                      stop=(k == NT - 1),
                                      skip_group_check=True),
                                  sem="CT", inc=(1 if k == NT - 1 else 0),
                                  produces=([("ups", tb, c),
                                             ("wgfree", tb, c)]
                                            if k == NT - 1 else []))
                        deps = [("gps", tb, c)]
                        if c >= 2:
                            deps.append(("hmul", tb, c - 2))
                        elif tb:
                            deps.append(("hmul", tb - 1, 62 + c))
                        em.op("scalar",
                              lambda e, j=j: e.activation(
                                  out=silb[j][:], in_=pb[0][:], func=AF.Silu),
                              deps=deps, sem="CS", inc=1,
                              produces=[("sil", tb, c)])
                        deps = [("sil", tb, c), ("ups", tb, c)]
                        if c >= 2:
                            deps.append(("hsq", tb, c - 2))
                        elif tb:
                            deps.append(("hsq", tb - 1, 62 + c))
                        em.op("vector",
                              lambda e, j=j: e.tensor_tensor(
                                  out=hfb[j][:], in0=silb[j][:], in1=pb[1][:],
                                  op=ALU.mult),
                              deps=deps, sem="CV", inc=1,
                              produces=[("hmul", tb, c)])
                        deps = [("hmul", tb, c)]
                        if q or tb:
                            deps.append(("mm2dn", (tb, q - 1) if q else
                                         (tb - 1, 3)))
                        em.op("vector",
                              lambda e, j=j, s=cc: e.tensor_copy(
                                  hsthi[:, s, :], hfb[j][:]),
                              deps=deps, sem="CV", inc=0)
                        em.op("vector",
                              lambda e, j=j, s=cc: e.tensor_tensor(
                                  out=hstlo[:, s, :], in0=hfb[j][:],
                                  in1=hsthi[:, s, :], op=ALU.subtract),
                              sem="CV", inc=1, produces=[("hst", tb, c)])
                        deps = [("hmul", tb, c)]
                        if c >= 2:
                            deps.append(("sqmm", tb, c - 2))
                        elif tb:
                            deps.append(("sqmm", tb - 1, 62 + c))
                        em.op("scalar",
                              lambda e, j=j: e.activation(
                                  out=sqb[j][:], in_=hfb[j][:],
                                  func=AF.Square),
                              deps=deps, sem="CS", inc=1,
                              produces=[("hsq", tb, c)])
                        deps = [("hsq", tb, c), ("onesb",)]
                        if c == 0 and tb:
                            deps.append(("rstd4ln", tb - 1))
                        em.op("tensor",
                              lambda e, j=j, c=c: e.matmul(
                                  out=pb[2][0:1, :], lhsT=ones_cbf[:],
                                  rhs=sqb[j][:], start=(c == 0),
                                  stop=(c == 63), skip_group_check=True),
                              deps=deps, sem="CT", inc=1,
                              produces=[("sqmm", tb, c)])
                    for tau in range(NT):
                        j = tau % 2
                        bank = pb[3 + j]
                        deps = [("wpfree", tb, q, tau - 2)] if tau >= 2 else (
                            [("wpfree", tb, q - 1, 14 + tau)] if q else (
                                [("wpfree", tb - 1, 3, 14 + tau)] if tb
                                else []))
                        em.op("sync",
                              lambda e, t=tau, q=q, j=j: e.dma_start(
                                  out=wp[j][:],
                                  in_=wpp_t[t, q * 16:(q + 1) * 16, :, :]
                                  .rearrange("c p h -> p c h")),
                              deps=deps, sem="L%d" % (6 + j), inc=16,
                              produces=[("wp4", tb, q, tau)], is_dma=True)
                        deps = [("wp4", tb, q, tau), ("hst", tb, q * 16 + 15)]
                        if tau >= 2:
                            deps.append(("gev", tb, q, tau - 2))
                        elif q:
                            deps.append(("gev", tb, q - 1, 14 + tau))
                        elif tb:
                            deps.append(("gev", tb - 1, 3, 14 + tau))
                        for cc in range(16):
                            em.op("tensor",
                                  lambda e, j=j, cc=cc, bank=bank: e.matmul(
                                      out=bank[:], lhsT=wp[j][:, cc, :],
                                      rhs=hsthi[:, cc, :], start=(cc == 0),
                                      stop=False, skip_group_check=True),
                                  deps=(deps if cc == 0 else []))
                            em.op("tensor",
                                  lambda e, j=j, cc=cc, bank=bank: e.matmul(
                                      out=bank[:], lhsT=wp[j][:, cc, :],
                                      rhs=hstlo[:, cc, :], start=False,
                                      stop=(cc == 15), skip_group_check=True),
                                  sem="CT", inc=(1 if cc == 15 else 0),
                                  produces=([("mm2", tb, q, tau),
                                             ("wpfree", tb, q, tau)]
                                            if cc == 15 else []))
                        if q == 0:
                            deps = [("mm2", tb, 0, tau)]
                            if tb:
                                deps.append(("gsl", tb - 1, tau))
                            em.op("vector",
                                  lambda e, t=tau, bank=bank: e.tensor_copy(
                                      gacc[:, t, :], bank[:]),
                                  deps=deps, sem="CV", inc=1,
                                  produces=[("gev", tb, 0, tau)])
                        else:
                            em.op("vector",
                                  lambda e, t=tau, bank=bank: e.tensor_tensor(
                                      out=gacc[:, t, :], in0=gacc[:, t, :],
                                      in1=bank[:], op=ALU.add),
                                  deps=[("mm2", tb, q, tau),
                                        ("gev", tb, q - 1, tau)],
                                  sem="CV", inc=1,
                                  produces=[("gev", tb, q, tau)])
                    em.alias(("mm2dn", (tb, q)), ("mm2", tb, q, NT - 1))
                em.alias(("mm1dn", tb), ("ups", tb, 63))

                em.op("scalar",
                      lambda e: e.activation(out=rb4[0:1, :], in_=pb[2][0:1, :],
                                             func=AF.Ln, scale=1.0 / 8192.0,
                                             bias=epsrm_c[:]),
                      deps=[("sqmm", tb, 63), ("epsrm",)] +
                      ([("rb4ev", tb - 1), ("gsc", tb - 1, NT - 1)]
                       if tb else []),
                      sem="CS", inc=1, produces=[("rstd4ln", tb)])
                em.op("scalar",
                      lambda e: e.activation(out=rb4[0:1, :], in_=rb4[0:1, :],
                                             func=AF.Exp, scale=-0.5),
                      sem="CS", inc=1, produces=[("rstd4row", tb)])
                em.op("tensor",
                      lambda e: e.matmul(out=pb[7][:], lhsT=ones_row[:],
                                         rhs=rb4[0:1, :], start=True, stop=True,
                                         skip_group_check=True),
                      deps=[("rstd4row", tb)] +
                      ([("rb4ev", tb - 1)] if tb else []),
                      sem="CT", inc=1, produces=[("rbps7", tb)])
                em.op("scalar", lambda e: e.copy(out=rb4[:], in_=pb[7][:]),
                      deps=[("rbps7", tb)], sem="CS", inc=1,
                      produces=[("rb4ev", tb), ("rb4bc", tb)])
                for tau in range(NT):
                    j = tau % 2
                    em.op("vector",
                          lambda e, t=tau: e.tensor_tensor(
                              out=gacc[:, t, :], in0=gacc[:, t, :], in1=rb4[:],
                              op=ALU.mult),
                          deps=[("gev", tb, 3, tau), ("rb4bc", tb)],
                          sem="CV", inc=1, produces=[("gsc", tb, tau)])
                    deps = [("gsc", tb, tau)]
                    if tau >= 2:
                        deps.append(("sg", tb, tau - 2))
                    elif tb:
                        deps.append(("sg", tb - 1, 14 + tau))
                    em.op("scalar",
                          lambda e, t=tau, j=j: e.activation(
                              out=gsil[j][:], in_=gacc[:, t, :], func=AF.Silu),
                          deps=deps, sem="CS", inc=1,
                          produces=[("gsl", tb, tau)])
                    deps = [("xn4s", tb, 14 + tau)] if tau < 2 else \
                        [("out", tb, tau - 2)]
                    em.op("sync",
                          lambda e, t=tau, co=tb * F, j=j: e.dma_start(
                              out=xab[j][:],
                              in_=xT_own[t * 128:(t + 1) * 128, co:co + F]),
                          deps=deps, sem="L%d" % j, inc=16,
                          produces=[("cx", tb, tau)], is_dma=True)
                    deps = [("sg", tb, tau - 2)] if tau >= 2 else (
                        [("sg", tb - 1, 14 + tau)] if tb else [])
                    em.op("sync",
                          lambda e, t=tau, j=j: e.dma_start(
                              out=ssb[j][:], in_=s_t[t, tb, :, :]),
                          deps=deps, sem="L%d" % (8 + j), inc=16,
                          produces=[("cs4", tb, tau)], is_dma=True)
                    deps = [("cs4", tb, tau), ("gsl", tb, tau)]
                    if tau >= 2:
                        deps.append(("out", tb, tau - 2))
                    elif tb:
                        deps.append(("out", tb - 1, 14 + tau))
                    em.op("gpsimd",
                          lambda e, j=j: e.tensor_tensor(
                              out=t1b[j][:], in0=ssb[j][:], in1=gsil[j][:],
                              op=ALU.mult),
                          deps=deps, sem="CP", inc=1, produces=[("sg", tb, tau)])
                    deps = [("sg", tb, tau), ("cx", tb, tau)]
                    if tau >= 2:
                        deps.append(("o_st", tb, tau - 2))
                    elif tb:
                        deps.append(("o_st", tb - 1, 14 + tau))
                    em.op("vector",
                          lambda e, j=j: e.tensor_tensor(
                              out=outb[j][:], in0=xab[j][:], in1=t1b[j][:],
                              op=ALU.add),
                          deps=deps, sem="CV", inc=1,
                          produces=[("out", tb, tau), ("comb_x", tb, tau)])
                    em.op("scalar",
                          lambda e, t=tau, co=tb * F, j=j: e.dma_start(
                              out=outT[t * 128:(t + 1) * 128, co:co + F],
                              in_=outb[j][:]),
                          deps=[("out", tb, tau)], sem="S%d" % (j % 2), inc=16,
                          produces=[("o_st", tb, tau)], is_dma=True)

        phases = [(alloc_p0, phase_p0), (alloc_p1, phase_p1),
                  (alloc_p2, phase_p2), (alloc_p3, phase_p3),
                  (alloc_p4, phase_p4)]
        em.snapshot()
        with nc.Block() as block:
            for alloc_fn, phase_fn in phases:
                with ExitStack() as pes:
                    bufs = alloc_fn(pes)
                    for ename in ENGINES:
                        sec = getattr(block, ename)

                        def body(eng, ename=ename, phase_fn=phase_fn,
                                 bufs=bufs):
                            em.begin_pass(ename, eng)
                            phase_fn(em, bufs)
                            em.barrier()
                        sec(body)
                    em.snapshot()
    if check:
        return nc, em
    return nc


_NC_CACHE = {}


def _get_nc(dbg=False):
    if dbg not in _NC_CACHE:
        _NC_CACHE[dbg] = build(dbg)
    return _NC_CACHE[dbg]


def make_in_maps(L0_attn_x, L0_attn_A_log, L0_attn_D, L0_attn_W_x,
                 L0_attn_W_dt, L0_attn_b_dt, L0_ffn_w_gate_up, L0_ffn_w_proj):
    x = np.asarray(L0_attn_x, dtype=np.float32)
    wguT = np.ascontiguousarray(np.asarray(L0_ffn_w_gate_up, np.float32).T)
    wpT = np.ascontiguousarray(np.asarray(L0_ffn_w_proj, np.float32).T)
    wxT3 = np.ascontiguousarray(
        np.asarray(L0_attn_W_x, np.float32).T.reshape(16, 128, 96)
        .transpose(1, 0, 2))
    wdtT = np.ascontiguousarray(np.asarray(L0_attn_W_dt, np.float32).T)
    alog3 = np.ascontiguousarray(
        np.asarray(L0_attn_A_log, np.float32).reshape(16, 128, 16)
        .transpose(1, 0, 2).reshape(128, 256))
    d3 = np.ascontiguousarray(
        np.asarray(L0_attn_D, np.float32).reshape(16, 128).T)
    bdt3 = np.ascontiguousarray(
        np.asarray(L0_attn_b_dt, np.float32).reshape(16, 128).T)
    selsum = np.zeros((128, 16, 128), np.float32)
    for j in range(16):
        for k in range(128):
            selsum[k, j, 2 * j + (k >= 64)] = 1.0
    selsum = selsum.astype(ml_dtypes.bfloat16)
    selbc = np.zeros((32, 16, 128), np.float32)
    for j in range(16):
        for m in range(128):
            selbc[2 * j + (m >= 64), j, m] = 1.0
    selbc = selbc.astype(ml_dtypes.bfloat16)

    in_maps = []
    for c in range(8):
        b, hf = c // 2, c % 2
        xT_pre = np.ascontiguousarray(x[b, 0:2048, :].T)
        xT_own = np.ascontiguousarray(x[b, 2048:4096, :].T) if hf else xT_pre
        flag = np.full((128, 256), float(hf), np.float32)
        in_maps.append({
            "xT_pre": xT_pre, "xT_own": xT_own, "wguT": wguT, "wpT": wpT,
            "wxT3": wxT3, "wdtT": wdtT, "alog3": alog3, "d3": d3,
            "bdt3": bdt3, "flagbc": flag, "selsum": selsum, "selbc": selbc,
        })
    return in_maps


def kernel(L0_attn_x, L0_attn_A_log, L0_attn_D, L0_attn_W_x, L0_attn_W_dt,
           L0_attn_b_dt, L0_ffn_w_gate_up, L0_ffn_w_proj):
    in_maps = make_in_maps(L0_attn_x, L0_attn_A_log, L0_attn_D, L0_attn_W_x,
                           L0_attn_W_dt, L0_attn_b_dt, L0_ffn_w_gate_up,
                           L0_ffn_w_proj)
    nc = _get_nc(False)
    res = run_bass_kernel_spmd(nc, in_maps, core_ids=list(range(8)))
    out = np.empty((4, 4096, 2048), np.float32)
    for c in range(8):
        b, hf = c // 2, c % 2
        out[b, hf * 2048:(hf + 1) * 2048, :] = res.results[c]["outT"].T
    return out
